# version 05
# baseline (speedup 1.0000x reference)
import numpy as np
import ml_dtypes
import concourse.bass as bass
import concourse.mybir as mybir
from concourse.alu_op_type import AluOpType
from concourse.bass_utils import run_bass_kernel_spmd
from contextlib import ExitStack

F32 = mybir.dt.float32
BF16 = mybir.dt.bfloat16
I8 = mybir.dt.int8
I32 = mybir.dt.int32
AF = mybir.ActivationFunctionType
AX = mybir.AxisListType
OPS = AluOpType

S, D, H, DK = 2048, 1024, 16, 64
STOP = 99
DBG = False
NC = 16
NP = 8
QT = 4
ENGS = ("sync", "tensor", "vector", "scalar", "gpsimd")
SHUF_SWAP = [j + 1 - 2 * (j % 2) for j in range(32)]


def host_tables():
    inv = 1.0 / 10000.0 ** (np.arange(0, DK, 2, dtype=np.float64) / DK)
    pos = np.arange(S, dtype=np.float64)
    th = pos[None, :] * inv[:, None]
    cos = np.repeat(np.cos(th), 2, axis=0)
    sin = np.repeat(np.sin(th), 2, axis=0)
    sgn = np.where(np.arange(DK) % 2 == 0, -1.0, 1.0)[:, None]
    cos_t = np.vstack([cos, cos]).astype(ml_dtypes.bfloat16)
    sin_t = np.vstack([sin * sgn, sin * sgn]).astype(ml_dtypes.bfloat16)
    identb = np.eye(128, dtype=ml_dtypes.bfloat16)
    onesf = np.ones((1, 128), dtype=np.float32)
    return cos_t, sin_t, identb, onesf


class Sem:
    def __init__(self, handle):
        self.h = handle
        self.n = 0


class Builder:
    def __init__(self):
        self.ops = {e: [] for e in ENGS}

    def w(self, eng, sem, val):
        if val <= 0:
            return
        h = sem.h
        self.ops[eng].append(lambda e, h=h, v=val: e.wait_ge(h, v))

    def ping(self, sems, prod_mark):
        dg = sems["dg"]
        dv = sems["dv"]
        self.w("gpsimd", dv, prod_mark)
        dg.n += 1
        h, v = dg.h, dg.n
        self.ops["gpsimd"].append(lambda e, h=h: e.sem_inc(h, 1))
        self.w("vector", dg, dg.n)

    def i(self, eng, fn, sem=None, k=1):
        if sem is None:
            self.ops[eng].append(fn)
            if eng == "vector":
                self.ops[eng].append(lambda e: e.drain())
            return None
        sem.n += k
        h, kk = sem.h, k
        self.ops[eng].append(lambda e, fn=fn, h=h, kk=kk: fn(e).then_inc(h, kk))
        if eng == "vector":
            self.ops[eng].append(lambda e: e.drain())
        return sem.n


def build_nc():
    nc = bass.Bass("TRN2", target_bir_lowering=False, debug=False)

    x_h = nc.dram_tensor("x", [S, D], F32, kind="ExternalInput").ap()
    wq_h = nc.dram_tensor("wq", [D, D], F32, kind="ExternalInput").ap()
    wk_h = nc.dram_tensor("wk", [D, D], F32, kind="ExternalInput").ap()
    wv_h = nc.dram_tensor("wv", [D, D], F32, kind="ExternalInput").ap()
    wo_h = nc.dram_tensor("wo", [D, D], F32, kind="ExternalInput").ap()
    wf8_h = nc.dram_tensor("wf8", [D, D], I8, kind="ExternalInput").ap()
    wsc_h = nc.dram_tensor("wsc", [1, D], F32, kind="ExternalInput").ap()
    wb_h = nc.dram_tensor("wb", [1, D], F32, kind="ExternalInput").ap()
    cos_h = nc.dram_tensor("cost", [128, S], BF16, kind="ExternalInput").ap()
    sin_h = nc.dram_tensor("sint", [128, S], BF16, kind="ExternalInput").ap()
    id_h = nc.dram_tensor("identb", [128, 128], BF16, kind="ExternalInput").ap()
    onesf_h = nc.dram_tensor("onesf", [1, 128], F32, kind="ExternalInput").ap()
    out_h = nc.dram_tensor("out", [S, D], F32, kind="ExternalOutput").ap()

    stg_q = nc.dram_tensor("stg_q", [D, D], BF16).ap()
    stg_k = nc.dram_tensor("stg_k", [D, D], BF16).ap()
    stg_v = nc.dram_tensor("stg_v", [D, D], BF16).ap()
    stg_o = nc.dram_tensor("stg_o", [D, D], BF16).ap()
    stg_f = nc.dram_tensor("stg_f", [D, D], BF16).ap()
    s2d = nc.dram_tensor("s2d", [S, D], F32).ap()
    dbg = {}
    if DBG:
        dbg["d_y1"] = nc.dram_tensor("d_y1", [128, 16384], BF16, kind="ExternalOutput").ap()
        dbg["d_vaug"] = nc.dram_tensor("d_vaug", [128, 16768], BF16, kind="ExternalOutput").ap()
        dbg["d_qp"] = nc.dram_tensor("d_qp", [128, 2048], BF16, kind="ExternalOutput").ap()
        dbg["d_ka"] = nc.dram_tensor("d_ka", [128, 2048], BF16, kind="ExternalOutput").ap()
        dbg["d_kb"] = nc.dram_tensor("d_kb", [128, 2048], BF16, kind="ExternalOutput").ap()
        dbg["d_ofm"] = nc.dram_tensor("d_ofm", [128, 16384], BF16, kind="ExternalOutput").ap()
        dbg["d_s2"] = nc.dram_tensor("d_s2", [S, D], F32, kind="ExternalOutput").ap()
        dbg["d_st"] = nc.dram_tensor("d_st", [128, 64], F32, kind="ExternalOutput").ap()
        dbg["d_svb"] = nc.dram_tensor("d_svb", [128, 1024], F32, kind="ExternalOutput").ap()
        dbg["d_bbb"] = nc.dram_tensor("d_bbb", [128, 1024], F32, kind="ExternalOutput").ap()
        dbg["d_qi"] = nc.dram_tensor("d_qi", [128, 1024], mybir.dt.int32, kind="ExternalOutput").ap()
        dbg["d_fb"] = nc.dram_tensor("d_fb", [128, 1024], F32, kind="ExternalOutput").ap()
        dbg["d_t1"] = nc.dram_tensor("d_t1", [128, 1024], F32, kind="ExternalOutput").ap()
    mxd = nc.dram_tensor("mxd", [128, 1], F32).ap()
    ccin = nc.dram_tensor("ccin", [1, 1], F32).ap()
    ccout = nc.dram_tensor("ccout", [1, 1], F32).ap()

    with ExitStack() as st:
        e = st.enter_context

        AB = {}
        cur = 0
        for name, ln in [("y1", 16384), ("vaug", 16768), ("cos", 2048), ("sin", 2048),
                         ("ka", 2048), ("kb", 2048), ("qp", 2048), ("wqp", 1024),
                         ("wkp", 1024), ("wbig", 8192), ("pt", 2048), ("scr", 2048),
                         ("ident", 128), ("ones128", 128), ("ofm", 16384)]:
            AB[name] = cur
            cur += ln
        ab = e(nc.sbuf_tensor("ab", [128, cur], BF16))

        def B(name, off, ln, p0=0, pn=128):
            return ab[p0:p0 + pn, AB[name] + off: AB[name] + off + ln]

        AF32 = {}
        cur = 0
        for name, ln in [("xb", 2048), ("s2b", 2048), ("t1", 1024), ("fb", 1024),
                         ("outb", 2048), ("svb", 1024), ("bbb", 1024),
                         ("srow", 3584), ("stats", 64)]:
            AF32[name] = cur
            cur += ln
        af = e(nc.sbuf_tensor("af", [128, cur], F32))

        def F(name, off, ln, p0=0, pn=128):
            return af[p0:p0 + pn, AF32[name] + off: AF32[name] + off + ln]

        qib = e(nc.sbuf_tensor("qib", [128, 1024], I32))


        PA = e(nc.psum_tensor("PA", [128, 1024], F32))
        PB = e(nc.psum_tensor("PB", [128, 1024], F32))
        PC = e(nc.psum_tensor("PC", [128, 512], F32))
        PD = e(nc.psum_tensor("PD", [128, 512], F32))
        PE = e(nc.psum_tensor("PE", [128, 512], F32))
        PT = e(nc.psum_tensor("PT", [128, 1024], BF16))

        sems = {}
        for s_ in ["q0", "qx0", "qx1", "qw", "qv", "qk", "qs2w", "qs2r0", "qs2r1",
                   "qo0", "qo1", "qm", "qwo", "qwf", "dt", "dv", "dsc", "dg", "ccs"]:
            sems[s_] = Sem(e(nc.semaphore(s_)))
        q0, qw, qv, qk = (sems[k] for k in ["q0", "qw", "qv", "qk"])
        qs2w, qm, qwo, qwf = (sems[k] for k in ["qs2w", "qm", "qwo", "qwf"])
        dt, dv, dsc, dg, ccs = (sems[k] for k in ["dt", "dv", "dsc", "dg", "ccs"])
        qx = [sems["qx0"], sems["qx1"]]
        qs2r = [sems["qs2r0"], sems["qs2r1"]]
        qo = [sems["qo0"], sems["qo1"]]

        b = Builder()

        eps_ap = F("stats", 17, 1)
        mva = F("stats", 0, 2)
        bna = F("stats", 2, 12)
        rinv = F("stats", 14, 1)
        nbv = F("stats", 15, 1)
        sdv = F("stats", 16, 1)
        onesf_row = F("srow", 2048, 128, pn=1)
        onesf_11 = F("srow", 2048, 1, pn=1)
        scrrow = F("srow", 2176, 1024, pn=1)


        def emit_ln_stats(src_fn, junk_name):
            b.i("vector", lambda e_: e_.tensor_reduce(
                out=mva[:, 0:1], in_=src_fn(), axis=AX.X, op=OPS.add))
            b.i("vector", lambda e_: e_.tensor_tensor(
                out=F(junk_name, 0, 1024), in0=src_fn(), in1=src_fn(), op=OPS.mult))
            b.i("vector", lambda e_: e_.tensor_reduce(
                out=mva[:, 1:2], in_=F(junk_name, 0, 1024), axis=AX.X, op=OPS.add))
            b.i("vector", lambda e_: e_.tensor_scalar(
                out=mva[:, 0:1], in0=mva[:, 0:1], scalar1=1.0 / 1024.0, scalar2=None,
                op0=OPS.mult, op1=OPS.bypass))
            b.i("vector", lambda e_: e_.tensor_scalar(
                out=mva[:, 1:2], in0=mva[:, 1:2], scalar1=1.0 / 1024.0, scalar2=None,
                op0=OPS.mult, op1=OPS.bypass))
            b.i("vector", lambda e_: e_.tensor_scalar(
                out=F("stats", 2, 1), in0=mva[:, 0:1], scalar1=-1.0, scalar2=None,
                op0=OPS.mult, op1=OPS.bypass))
            b.i("vector", lambda e_: e_.tensor_tensor(
                out=nbv, in0=mva[:, 0:1], in1=mva[:, 0:1], op=OPS.mult))
            return b.i("vector", lambda e_: e_.tensor_tensor(
                out=mva[:, 1:2], in0=mva[:, 1:2], in1=nbv, op=OPS.subtract), dv)

        def emit_ln_tail():
            b.i("vector", lambda e_: e_.reciprocal(out=rinv, in_=sdv))
            mk = b.i("vector", lambda e_: e_.tensor_tensor(
                out=nbv, in0=F("stats", 2, 1), in1=rinv, op=OPS.mult), dv)
            b.ping(sems, mk)

        b.i("sync", lambda e_: e_.dma_start(out=B("cos", 0, S), in_=cos_h), q0, 16)
        b.i("sync", lambda e_: e_.dma_start(out=B("sin", 0, S), in_=sin_h), q0, 16)
        b.i("sync", lambda e_: e_.dma_start(out=B("ident", 0, 128), in_=id_h), q0, 16)
        b.i("sync", lambda e_: e_.dma_start(out=onesf_row, in_=onesf_h), q0, 16)
        b.i("sync", lambda e_: e_.dma_start(out=F("srow", 0, 1024, pn=1), in_=wsc_h), q0, 16)
        b.i("sync", lambda e_: e_.dma_start(out=F("srow", 1024, 1024, pn=1), in_=wb_h), q0, 16)

        b.i("gpsimd", lambda e_: e_.dma_start(out=stg_q, in_=wq_h), qw, 16)
        b.i("gpsimd", lambda e_: e_.dma_start(out=stg_k, in_=wk_h), qw, 16)
        b.i("gpsimd", lambda e_: e_.dma_start(out=stg_v, in_=wv_h), qw, 16)
        b.i("gpsimd", lambda e_: e_.dma_start(out=stg_o, in_=wo_h), qw, 16)
        b.i("gpsimd", lambda e_: e_.dma_start(out=stg_f, in_=wf8_h), qw, 16)

        b.i("vector", lambda e_: e_.memset(B("ka", 0, 2048), 0.0))
        b.i("vector", lambda e_: e_.memset(B("kb", 0, 2048), 0.0))
        b.i("vector", lambda e_: e_.memset(B("vaug", 0, 16768), 0.0))
        vaug3 = ab[:, AB["vaug"]:AB["vaug"] + 16640].rearrange("p (t h c) -> p t h c", h=16, c=65)
        b.i("vector", lambda e_: e_.memset(vaug3[:, :, :, 64:65], 1.0))
        b.i("vector", lambda e_: e_.memset(B("ones128", 0, 128), 1.0))
        b.i("vector", lambda e_: e_.memset(F("stats", 17, 1), 1e-5))
        b.i("vector", lambda e_: e_.memset(B("scr", 512, 640), 0.0))
        b.i("vector", lambda e_: e_.memset(F("stats", 21, 1), 0.0), dv)

        b.w("sync", qw, 48)
        for c in range(8):
            b.i("sync", lambda e_, c=c: e_.dma_start_transpose(
                out=B("wbig", 1024 * c, 1024), in_=stg_v[:, 128 * c:128 * c + 128]), qv, 16)

        if STOP < 1:
            if DBG:
                _emit_dbg(nc, b, e, dbg, ab, af, AB, AF32, s2d, sems, qib)
            return _install(nc, b, e)
        norm_marks = {}
        evac_marks = {}
        vproj_marks = {}
        tp_marks = {}
        for t in range(NC):
            slot = t % 2
            if t >= 2:
                b.w("sync", dv, norm_marks[t - 2])
            b.i("sync", lambda e_, t=t, slot=slot: e_.dma_start(
                out=F("xb", 1024 * slot, 1024), in_=x_h[128 * t:128 * t + 128, :]), qx[slot], 16)

            b.w("vector", qx[slot], 16 * (t // 2 + 1))
            m_ag = emit_ln_stats(lambda slot=slot: F("xb", 1024 * slot, 1024), "svb")
            b.w("scalar", dv, m_ag)
            m_sq = b.i("scalar", lambda e_: e_.activation(out=sdv, in_=mva[:, 1:2], func=AF.Sqrt,
                                                          bias=eps_ap, scale=1.0), dsc)
            b.w("vector", dsc, m_sq)
            emit_ln_tail()
            if t >= 2:
                b.w("vector", dt, tp_marks[t - 2])
            norm_marks[t] = b.i("vector", lambda e_, slot=slot: e_.tensor_scalar(
                out=B("scr", 1024 * slot, 1024), in0=F("xb", 1024 * slot, 1024),
                scalar1=rinv[:, 0:1], scalar2=nbv[:, 0:1], op0=OPS.mult, op1=OPS.add), dv)

            b.w("tensor", dv, norm_marks[t])
            if t >= 1:
                b.w("tensor", dv, evac_marks[t - 1])
            m_tp = None
            for c in range(8):
                m_tp = b.i("tensor", lambda e_, c=c, slot=slot: e_.transpose(
                    PT[:, 128 * c:128 * c + 128], B("scr", 1024 * slot + 128 * c, 128),
                    B("ident", 0, 128)), dt)
            tp_marks[t] = m_tp
            b.w("vector", dt, m_tp)
            y1_dst = ab[:, AB["y1"]:AB["y1"] + 16384].rearrange(
                "p (c s) -> p c s", s=2048)[:, :, 128 * t:128 * t + 128]
            evac_marks[t] = b.i("vector", lambda e_, y1_dst=y1_dst: e_.tensor_copy(
                out=y1_dst, in_=PT[:, 0:1024].rearrange("p (c s) -> p c s", s=128)), dv)

            b.w("tensor", qv, 128)
            b.w("tensor", dv, evac_marks[t])
            if t >= 1:
                b.w("tensor", dv, vproj_marks[t - 1])
            mk = None
            for oh in range(2):
                bank = PC if oh == 0 else PD
                for c in range(8):
                    mk = b.i("tensor", lambda e_, bank=bank, c=c, t=t, oh=oh: e_.matmul(
                        bank[:, :],
                        ab[:, AB["y1"] + 2048 * c + 128 * t: AB["y1"] + 2048 * c + 128 * t + 128],
                        B("wbig", 1024 * c + 512 * oh, 512),
                        start=(c == 0), stop=(c == 7)), dt)
            b.w("vector", dt, mk)
            vdst0 = ab[:, AB["vaug"] + 1040 * t: AB["vaug"] + 1040 * t + 520].rearrange(
                "p (h c) -> p h c", c=65)[:, :, 0:64]
            vdst1 = ab[:, AB["vaug"] + 1040 * t + 520: AB["vaug"] + 1040 * t + 1040].rearrange(
                "p (h c) -> p h c", c=65)[:, :, 0:64]
            b.i("vector", lambda e_, vdst0=vdst0: e_.tensor_copy(out=vdst0, in_=PC[:, :]))
            vproj_marks[t] = b.i("vector", lambda e_, vdst1=vdst1: e_.tensor_copy(out=vdst1, in_=PD[:, :]), dv)

        y1_done = evac_marks[NC - 1]
        ph1_t_done = dt.n
        ph1_v_done = dv.n

        b.w("sync", dt, ph1_t_done)
        for c in range(8):
            b.i("sync", lambda e_, c=c: e_.dma_start_transpose(
                out=B("wbig", 1024 * c, 1024), in_=stg_o[:, 128 * c:128 * c + 128]), qwo, 16)

        if STOP < 2:
            if DBG:
                _emit_dbg(nc, b, e, dbg, ab, af, AB, AF32, s2d, sems, qib)
            return _install(nc, b, e)
        cva = F("stats", 22, 1)
        cvb = F("stats", 23, 1)
        rowmax = F("stats", 24, 1)
        rdt4 = F("stats", 28, 4)

        last_exp = {0: 0, 1: 0}
        last_ptil = {0: 0, 1: 0}
        last_obank = {0: ph1_v_done, 1: ph1_v_done}
        last_pt_free = ph1_v_done
        last_pe_free = ph1_v_done
        cv_marks = [0, 0]
        final_proj_done = 0

        b.w("sync", qw, 80)
        for wreg, stg in (("wqp", stg_q), ("wkp", stg_k)):
            for c in range(8):
                b.i("sync", lambda e_, c=c, wreg=wreg, stg=stg: e_.dma_start_transpose(
                    out=B(wreg, 128 * c, 128), in_=stg[0:128, 128 * c:128 * c + 128]), qk, 16)

        for p in range(NP):
            b.w("tensor", qk, 256 * (p + 1))
            b.w("tensor", dv, y1_done)
            b.w("tensor", dv, last_pe_free)
            for ti, wreg in (("q", "wqp"), ("k", "wkp")):
                for sl in range(4):
                    if dg.n > 0:
                        b.w("tensor", dg, dg.n)
                    mk = None
                    for c in range(8):
                        mk = b.i("tensor", lambda e_, c=c, sl=sl, wreg=wreg: e_.matmul(
                            PE[:, :], B(wreg, 128 * c, 128),
                            ab[:, AB["y1"] + 2048 * c + 512 * sl: AB["y1"] + 2048 * c + 512 * sl + 512],
                            start=(c == 0), stop=(c == 7)), dt)
                    b.w("vector", dt, mk)
                    b.i("vector", lambda e_, sl=sl: e_.tensor_tensor(
                        out=F("xb", 0, 512), in0=PE[:, :], in1=B("cos", 512 * sl, 512), op=OPS.mult))
                    b.i("vector", lambda e_: e_.stream_shuffle(
                        out=F("xb", 512, 512), in_=PE[:, :], mask=SHUF_SWAP))
                    mk2 = b.i("vector", lambda e_, sl=sl: e_.tensor_tensor(
                        out=F("xb", 1024, 512), in0=F("xb", 512, 512),
                        in1=B("sin", 512 * sl, 512), op=OPS.mult), dv)
                    b.w("gpsimd", dv, mk2)
                    if ti == "q":
                        b.i("gpsimd", lambda e_, sl=sl: e_.tensor_tensor(
                            out=B("qp", 512 * sl, 512), in0=F("xb", 0, 512),
                            in1=F("xb", 1024, 512), op=OPS.add), dg)
                    else:
                        b.i("gpsimd", lambda e_, sl=sl: e_.tensor_tensor(
                            out=B("ka", 512 * sl, 512, p0=0, pn=64),
                            in0=F("xb", 0, 512, p0=0, pn=64),
                            in1=F("xb", 1024, 512, p0=0, pn=64), op=OPS.add))
                        b.i("gpsimd", lambda e_, sl=sl: e_.tensor_tensor(
                            out=B("kb", 512 * sl, 512, p0=64, pn=64),
                            in0=F("xb", 0, 512, p0=64, pn=64),
                            in1=F("xb", 1024, 512, p0=64, pn=64), op=OPS.add), dg)
            proj_done = dg.n
            final_proj_done = proj_done

            if p + 1 < NP:
                b.w("sync", dg, proj_done)
                for wreg, stg in (("wqp", stg_q), ("wkp", stg_k)):
                    for c in range(8):
                        b.i("sync", lambda e_, c=c, wreg=wreg, stg=stg, p=p: e_.dma_start_transpose(
                            out=B(wreg, 128 * c, 128),
                            in_=stg[128 * (p + 1):128 * (p + 1) + 128, 128 * c:128 * c + 128]), qk, 16)

            for h01 in range(2):
                kreg = "ka" if h01 == 0 else "kb"
                cv_ = cva if h01 == 0 else cvb
                b.w("tensor", dg, proj_done)
                b.w("tensor", dv, last_pt_free)
                b.w("tensor", dv, last_pe_free)
                mk = b.i("tensor", lambda e_, kreg=kreg: e_.matmul(
                    PE[:, :], B(kreg, 0, 128), B("qp", 0, 512), start=True, stop=True), dt)
                b.w("vector", dt, mk)
                b.i("vector", lambda e_: e_.reduce_max(out=rowmax, in_=PE[:, :], axis=AX.X))
                mk2 = b.i("vector", lambda e_: e_.tensor_copy(out=B("scr", 1216, 1), in_=rowmax), dv)
                b.w("tensor", dv, mk2)
                mk3 = b.i("tensor", lambda e_: e_.transpose(
                    PT[0:1, 0:128], B("scr", 1216, 1), B("ident", 0, 128)), dt)
                b.w("vector", dt, mk3)
                mkc = b.i("vector", lambda e_: e_.reduce_max(out=F("stats", 25, 1, pn=1),
                                                              in_=PT[0:1, 0:128], axis=AX.X), dv)
                b.ping(sems, mkc)
                mk4 = b.i("vector", lambda e_: e_.tensor_scalar(
                    out=F("srow", 2176, 128, pn=1), in0=onesf_row,
                    scalar1=F("stats", 25, 1, pn=1), scalar2=None,
                    op0=OPS.mult, op1=OPS.bypass), dv)
                last_pt_free = mk4
                b.w("tensor", dv, mk4)
                mk5 = b.i("tensor", lambda e_: e_.transpose(
                    PE[:, 0:1], F("srow", 2176, 128, pn=1), onesf_11), dt)
                b.w("vector", dt, mk5)
                cv_marks[h01] = b.i("vector", lambda e_, cv_=cv_: e_.tensor_scalar(
                    out=cv_, in0=PE[:, 0:1], scalar1=-0.125, scalar2=-4.0,
                    op0=OPS.mult, op1=OPS.add), dv)
                last_pe_free = cv_marks[h01]

            for h01 in range(2):
                kreg = "ka" if h01 == 0 else "kb"
                cv_ = cva if h01 == 0 else cvb
                hh = 2 * p + h01
                for qt in range(QT):
                    ob = qt % 2
                    obank = PC if ob == 0 else PD
                    exp_marks = {}
                    pv_marks = {}
                    for g in range(8):
                        sb_i = g % 2
                        sbuf = PA if sb_i == 0 else PB
                        pbuf_off = 1024 * sb_i
                        b.w("tensor", dsc, last_exp[sb_i])
                        for half in range(2):
                            kc = 2 * g + half
                            b.i("tensor", lambda e_, sbuf=sbuf, half=half, kc=kc, kreg=kreg, qt=qt: e_.matmul(
                                sbuf[:, 512 * half:512 * half + 512],
                                B(kreg, 128 * kc, 128), B("qp", 512 * qt, 512),
                                start=True, stop=True), dt)
                        sc_mark = dt.n
                        b.w("scalar", dt, sc_mark)
                        b.w("scalar", dt, last_ptil[sb_i])
                        b.w("scalar", dv, cv_marks[h01])
                        exp_marks[g] = b.i("scalar", lambda e_, sbuf=sbuf, pbuf_off=pbuf_off, cv_=cv_: e_.activation(
                            out=B("pt", pbuf_off, 1024), in_=sbuf[:, :], func=AF.Exp,
                            bias=cv_[:, 0:1], scale=0.125), dsc)
                        last_exp[sb_i] = exp_marks[g]
                        b.w("tensor", dsc, exp_marks[g])
                        if g == 0:
                            b.w("tensor", dv, last_obank[ob])
                        for half in range(2):
                            kcc = 2 * g + half
                            b.i("tensor", lambda e_, obank=obank, kcc=kcc, hh=hh, pbuf_off=pbuf_off, half=half: e_.matmul(
                                obank[:, :],
                                ab[:, AB["vaug"] + 1040 * kcc + 65 * hh: AB["vaug"] + 1040 * kcc + 65 * hh + 128],
                                B("pt", pbuf_off + 512 * half, 512),
                                start=(kcc == 0), stop=(kcc == 15)), dt)
                        pv_marks[g] = dt.n
                        last_ptil[sb_i] = pv_marks[g]

                    b.w("vector", dt, pv_marks[7])
                    osb_mark = b.i("vector", lambda e_, obank=obank: e_.tensor_copy(
                        out=B("scr", 0, 512, pn=65), in_=obank[0:65, :]), dv)
                    last_obank[ob] = osb_mark
                    b.w("tensor", dv, osb_mark)
                    b.w("tensor", dv, last_pt_free)
                    mk = None
                    for j in range(4):
                        mk = b.i("tensor", lambda e_, j=j: e_.transpose(
                            PT[:, 128 * j:128 * j + 65], B("scr", 128 * j, 128, pn=65),
                            B("ident", 0, 128, pn=65)[:, 0:65]), dt)
                    b.w("vector", dt, mk)
                    ot_sl = PT[:, 0:512].rearrange("p (j c) -> p j c", c=128)[:, :, 64:65]
                    mkr = b.i("vector", lambda e_, ot_sl=ot_sl: e_.reciprocal(
                        out=rdt4.rearrange("p (j c) -> p j c", c=1), in_=ot_sl), dv)
                    b.ping(sems, mkr)
                    for j in range(4):
                        b.i("vector", lambda e_, j=j: e_.tensor_scalar(
                            out=B("scr", 1216 + 128 * j, 128), in0=B("ident", 0, 128),
                            scalar1=rdt4[:, j:j + 1], scalar2=None, op0=OPS.mult, op1=OPS.bypass))
                    mk2 = b.i("vector", lambda e_: e_.tensor_copy(
                        out=B("scr", 512 + 64, 512), in_=PT[:, 0:512]), dv)
                    last_pt_free = mk2
                    b.w("tensor", dv, mk2)
                    b.w("tensor", dv, last_pe_free)
                    mk3 = None
                    for j in range(4):
                        off = 512 + 64 + 128 * j - 64 * h01
                        mk3 = b.i("tensor", lambda e_, j=j, off=off: e_.matmul(
                            PE[:, 128 * j: 128 * j + 128],
                            B("scr", off, 128), B("scr", 1216 + 128 * j, 128),
                            start=True, stop=True), dt)
                    b.w("vector", dt, mk3)
                    po = 64 * h01
                    last_pe_free = b.i("vector", lambda e_, po=po, p=p, qt=qt: e_.tensor_copy(
                        out=ab[po:po + 64, AB["ofm"] + 2048 * p + 512 * qt: AB["ofm"] + 2048 * p + 512 * qt + 512],
                        in_=PE[po:po + 64, 0:512]), dv)

        attn_done_v = dv.n
        attn_done_t = dt.n

        if STOP < 3:
            if DBG:
                _emit_dbg(nc, b, e, dbg, ab, af, AB, AF32, s2d, sems, qib)
            return _install(nc, b, e)
        sm2 = F("stats", 32, 16)
        srv2 = F("stats", 48, 16)
        mx_ = F("stats", 18, 1)
        mn_ = F("stats", 19, 1)
        dev_ = F("stats", 20, 1)
        lmax_ = F("stats", 21, 1)

        b.w("tensor", qwo, 128)
        b.w("tensor", dv, attn_done_v)
        b.w("sync", dg, final_proj_done)
        ph3_marks = {}
        t1_marks = {}
        for t in range(NC):
            slot = t % 2
            if t >= 1:
                for en in ("tensor", "vector", "scalar", "sync"):
                    b.w(en, dv, ph3_marks[t - 1])
                    b.w(en, dsc, dsc.n)
            if t >= 2:
                b.w("sync", dv, ph3_marks[t - 2])
            b.i("sync", lambda e_, t=t, slot=slot: e_.dma_start(
                out=F("xb", 1024 * slot, 1024), in_=x_h[128 * t:128 * t + 128, :]), qx[slot], 16)
            if t >= 1:
                b.w("tensor", dv, t1_marks[t - 1])
            mk = None
            for oh in range(2):
                bank = PC if oh == 0 else PD
                for c in range(8):
                    mk = b.i("tensor", lambda e_, bank=bank, c=c, t=t, oh=oh: e_.matmul(
                        bank[:, :],
                        ab[:, AB["ofm"] + 2048 * c + 128 * t: AB["ofm"] + 2048 * c + 128 * t + 128],
                        B("wbig", 1024 * c + 512 * oh, 512),
                        start=(c == 0), stop=(c == 7)), dt)
            b.w("vector", dt, mk)
            b.i("vector", lambda e_: e_.tensor_copy(out=F("t1", 0, 512), in_=PC[:, :]))
            t1_marks[t] = b.i("vector", lambda e_: e_.tensor_copy(out=F("t1", 512, 512), in_=PD[:, :]), dv)
            m_ag = emit_ln_stats(lambda: F("t1", 0, 1024), "svb")
            b.w("scalar", dv, m_ag)
            m_sq = b.i("scalar", lambda e_: e_.activation(out=sdv, in_=mva[:, 1:2], func=AF.Sqrt,
                                                          bias=eps_ap, scale=1.0), dsc)
            b.w("vector", dsc, m_sq)
            emit_ln_tail()
            b.i("vector", lambda e_: e_.tensor_scalar(
                out=F("t1", 0, 1024), in0=F("t1", 0, 1024), scalar1=rinv[:, 0:1], scalar2=nbv[:, 0:1],
                op0=OPS.mult, op1=OPS.add))
            b.w("vector", qx[slot], 16 * (qx[slot].n // 16))
            if t >= 2:
                b.w("vector", qs2w, 16 * (t - 1))
            b.i("vector", lambda e_, slot=slot: e_.tensor_tensor(
                out=F("s2b", 1024 * slot, 1024), in0=F("t1", 0, 1024),
                in1=F("xb", 1024 * slot, 1024), op=OPS.add))
            m_ag2 = emit_ln_stats(lambda slot=slot: F("s2b", 1024 * slot, 1024), "svb")
            b.w("scalar", dv, m_ag2)
            m_sq2 = b.i("scalar", lambda e_: e_.activation(out=sdv, in_=mva[:, 1:2], func=AF.Sqrt,
                                                           bias=eps_ap, scale=1.0), dsc)
            b.w("vector", dsc, m_sq2)
            emit_ln_tail()
            b.i("vector", lambda e_, t=t: e_.tensor_copy(out=sm2[:, t:t + 1], in_=mva[:, 0:1]))
            b.i("vector", lambda e_, t=t: e_.tensor_copy(out=srv2[:, t:t + 1], in_=rinv))
            b.i("vector", lambda e_, slot=slot: e_.tensor_scalar(
                out=F("svb", 0, 1024), in0=F("s2b", 1024 * slot, 1024),
                scalar1=rinv[:, 0:1], scalar2=nbv[:, 0:1], op0=OPS.mult, op1=OPS.add))
            b.i("vector", lambda e_: e_.reduce_max(out=dev_, in_=F("svb", 0, 1024), axis=AX.X,
                                                   apply_absolute_value=True))
            ph3_marks[t] = b.i("vector", lambda e_: e_.tensor_tensor(out=lmax_, in0=lmax_, in1=dev_, op=OPS.max), dv)
            b.w("sync", dv, ph3_marks[t])
            b.i("sync", lambda e_, t=t, slot=slot: e_.dma_start(
                out=s2d[128 * t:128 * t + 128, :], in_=F("s2b", 1024 * slot, 1024)), qs2w, 16)

        if STOP == 21:
            if DBG:
                _emit_dbg(nc, b, e, dbg, ab, af, AB, AF32, s2d, sems, qib)
            return _install(nc, b, e)
        b.w("sync", dv, ph3_marks[NC - 1])
        b.i("sync", lambda e_: e_.dma_start(out=mxd, in_=lmax_), qm, 16)
        b.w("sync", qm, 16)
        b.i("sync", lambda e_: e_.dma_start(out=F("srow", 3200, 128, pn=1),
                                            in_=mxd.rearrange("p one -> one p")), qm, 16)
        b.w("vector", qm, 32)
        b.i("vector", lambda e_: e_.reduce_max(out=F("stats", 25, 1, pn=1),
                                               in_=F("srow", 3200, 128, pn=1), axis=AX.X))
        mloc = b.i("vector", lambda e_: e_.tensor_scalar(
            out=F("stats", 25, 1, pn=1), in0=F("stats", 25, 1, pn=1),
            scalar1=1.0 / 127.0, scalar2=1e-5, op0=OPS.mult, op1=OPS.max), dv)
        b.w("sync", dv, mloc)
        b.i("sync", lambda e_: e_.dma_start(out=ccin, in_=F("stats", 25, 1, pn=1)), qm, 16)
        b.w("gpsimd", qm, 48)
        b.i("gpsimd", lambda e_: e_.collective_compute(
            "AllReduce", OPS.max, replica_groups=[list(range(8))],
            ins=[ccin.opt()], outs=[ccout.opt()]), ccs, 1)
        b.w("sync", ccs, 1)
        b.i("sync", lambda e_: e_.dma_start(out=F("stats", 25, 1, pn=1), in_=ccout), qm, 16)
        b.w("vector", qm, 64)
        mk = b.i("vector", lambda e_: e_.tensor_scalar(
            out=F("srow", 2176, 128, pn=1), in0=onesf_row,
            scalar1=F("stats", 25, 1, pn=1), scalar2=None, op0=OPS.mult, op1=OPS.bypass), dv)
        b.w("tensor", dv, mk)
        b.w("tensor", dv, attn_done_v)
        mk2 = b.i("tensor", lambda e_: e_.transpose(
            PE[:, 0:1], F("srow", 2176, 128, pn=1), onesf_11), dt)
        b.w("vector", dt, mk2)
        b.i("vector", lambda e_: e_.tensor_copy(out=F("stats", 26, 1), in_=PE[:, 0:1]))
        mkx = b.i("vector", lambda e_: e_.reciprocal(out=F("stats", 27, 1), in_=F("stats", 26, 1)), dv)
        b.ping(sems, mkx)

        if STOP == 22:
            if DBG:
                _emit_dbg(nc, b, e, dbg, ab, af, AB, AF32, s2d, sems, qib)
            return _install(nc, b, e)
        b.i("vector", lambda e_: e_.memset(B("scr", 1024, 1024), 0.0))
        sv_marks = {}
        for idx, dstname in ((0, "svb"), (1, "bbb")):
            if idx == 0:
                b.i("vector", lambda e_: e_.tensor_scalar(
                    out=scrrow, in0=F("srow", 0, 1024, pn=1),
                    scalar1=F("stats", 26, 1, pn=1), scalar2=None, op0=OPS.mult, op1=OPS.bypass))
            else:
                b.i("vector", lambda e_: e_.tensor_copy(out=scrrow, in_=F("srow", 1024, 1024, pn=1)))
            mk_hi = b.i("vector", lambda e_: e_.tensor_copy(out=B("scr", 1024, 1024, pn=1), in_=scrrow), dv)
            b.w("tensor", dv, mk_hi)
            if idx == 1:
                b.w("tensor", dv, sv_marks[0])
            mm_hi = None
            for oh in range(2):
                bank = PC if oh == 0 else PD
                mm_hi = b.i("tensor", lambda e_, bank=bank, oh=oh: e_.matmul(
                    bank[:, :], B("ones128", 0, 128), B("scr", 1024 + 512 * oh, 512),
                    start=True, stop=False), dt)
            b.w("vector", dt, mm_hi)
            b.i("vector", lambda e_: e_.tensor_tensor(
                out=scrrow, in0=scrrow, in1=B("scr", 1024, 1024, pn=1), op=OPS.subtract))
            mk_lo = b.i("vector", lambda e_: e_.tensor_copy(out=B("scr", 1024, 1024, pn=1), in_=scrrow), dv)
            b.w("tensor", dv, mk_lo)
            mm_lo = None
            for oh in range(2):
                bank = PC if oh == 0 else PD
                mm_lo = b.i("tensor", lambda e_, bank=bank, oh=oh: e_.matmul(
                    bank[:, :], B("ones128", 0, 128), B("scr", 1024 + 512 * oh, 512),
                    start=False, stop=True), dt)
            b.w("vector", dt, mm_lo)
            b.i("vector", lambda e_, dstname=dstname: e_.tensor_copy(out=F(dstname, 0, 512), in_=PC[:, :]))
            sv_marks[idx] = b.i("vector", lambda e_, dstname=dstname: e_.tensor_copy(
                out=F(dstname, 512, 512), in_=PD[:, :]), dv)

        sv_done = sv_marks[1]

        b.w("sync", dt, dt.n)
        for c in range(8):
            b.i("sync", lambda e_, c=c: e_.dma_start_transpose(
                out=B("wbig", 1024 * c, 1024), in_=stg_f[:, 128 * c:128 * c + 128]), qwf, 16)

        if STOP < 4:
            return _install(nc, b, e)
        qsc = F("stats", 24, 1)
        qbs = F("stats", 22, 1)
        qf_marks = {}
        ffn_mm_marks = {}
        stt_marks = {}
        out_marks = {}
        qt_marks = {}
        b.w("sync", qs2w, 256)
        for t in range(NC):
            slot = t % 2
            if t >= 1:
                for en in ("tensor", "vector", "scalar", "sync", "gpsimd"):
                    b.w(en, dg, out_marks[t - 1])
                    b.w(en, dv, dv.n)
                    b.w(en, dsc, dsc.n)
            if t >= 2:
                b.w("sync", dg, out_marks[t - 2])
            b.i("sync", lambda e_, t=t, slot=slot: e_.dma_start(
                out=F("s2b", 1024 * slot, 1024), in_=s2d[128 * t:128 * t + 128, :]), qs2r[slot], 16)
            b.w("vector", qs2r[slot], 16 * (t // 2 + 1))
            b.w("vector", dv, sv_done)
            b.i("vector", lambda e_, t=t: e_.tensor_tensor(out=qsc, in0=srv2[:, t:t + 1],
                                                           in1=F("stats", 27, 1), op=OPS.mult))
            mkq = b.i("vector", lambda e_, t=t: e_.tensor_tensor(out=qbs, in0=sm2[:, t:t + 1],
                                                                 in1=qsc, op=OPS.mult), dv)
            b.ping(sems, mkq)
            b.i("vector", lambda e_, slot=slot: e_.tensor_scalar(
                out=qib[:, :], in0=F("s2b", 1024 * slot, 1024), scalar1=qsc[:, 0:1],
                scalar2=qbs[:, 0:1], op0=OPS.mult, op1=OPS.subtract))
            if t >= 1:
                b.w("vector", dt, qt_marks[t - 1])
            mk = b.i("vector", lambda e_: e_.tensor_copy(out=B("scr", 0, 1024), in_=qib[:, :]), dv)
            b.w("tensor", dv, mk)
            if t >= 1:
                b.w("tensor", dv, qf_marks[t - 1])
            mk2 = None
            for c in range(8):
                mk2 = b.i("tensor", lambda e_, c=c: e_.transpose(
                    PT[:, 128 * c:128 * c + 128], B("scr", 128 * c, 128), B("ident", 0, 128)), dt)
            qt_marks[t] = mk2
            b.w("vector", dt, mk2)
            if t >= 2:
                b.w("vector", dt, ffn_mm_marks[t - 2])
            qf_marks[t] = b.i("vector", lambda e_, slot=slot: e_.tensor_copy(
                out=B("pt", 1024 * slot, 1024), in_=PT[:, 0:1024]), dv)
            b.w("tensor", dv, qf_marks[t])
            b.w("tensor", qwf, 128)
            if t == 0:
                b.w("tensor", dv, sv_done)
            if t >= 1:
                b.w("tensor", dv, stt_marks[t - 1])
            mk3 = None
            for oh in range(2):
                bank = PC if oh == 0 else PD
                for c in range(8):
                    mk3 = b.i("tensor", lambda e_, bank=bank, c=c, oh=oh, slot=slot: e_.matmul(
                        bank[:, :], B("pt", 1024 * slot + 128 * c, 128),
                        B("wbig", 1024 * c + 512 * oh, 512),
                        start=(c == 0), stop=(c == 7)), dt)
            ffn_mm_marks[t] = mk3
            b.w("vector", dt, mk3)
            if t >= 1:
                b.w("vector", dg, out_marks[t - 1])
            b.i("vector", lambda e_: e_.scalar_tensor_tensor(
                out=F("fb", 0, 512), in0=PC[:, :], scalar=1.0, in1=F("svb", 0, 512),
                op0=OPS.mult, op1=OPS.mult))
            stt_marks[t] = b.i("vector", lambda e_: e_.scalar_tensor_tensor(
                out=F("fb", 512, 512), in0=PD[:, :], scalar=1.0, in1=F("svb", 512, 512),
                op0=OPS.mult, op1=OPS.mult), dv)
            b.w("gpsimd", dv, stt_marks[t])
            mk5 = b.i("gpsimd", lambda e_: e_.tensor_tensor(
                out=F("t1", 0, 1024), in0=F("fb", 0, 1024), in1=F("bbb", 0, 1024), op=OPS.add), dg)
            b.w("vector", dg, mk5)
            m_ag = emit_ln_stats(lambda: F("t1", 0, 1024), "xb")
            b.w("scalar", dv, m_ag)
            m_sq = b.i("scalar", lambda e_: e_.activation(out=sdv, in_=mva[:, 1:2], func=AF.Sqrt,
                                                          bias=eps_ap, scale=1.0), dsc)
            b.w("vector", dsc, m_sq)
            emit_ln_tail()
            mk6 = b.i("vector", lambda e_: e_.tensor_scalar(
                out=F("fb", 0, 1024), in0=F("t1", 0, 1024), scalar1=rinv[:, 0:1],
                scalar2=nbv[:, 0:1], op0=OPS.mult, op1=OPS.add), dv)
            b.w("gpsimd", dv, mk6)
            if t >= 2:
                b.w("gpsimd", qo[slot], 16 * (t // 2))
            out_marks[t] = b.i("gpsimd", lambda e_, slot=slot: e_.tensor_tensor(
                out=F("outb", 1024 * slot, 1024), in0=F("fb", 0, 1024),
                in1=F("s2b", 1024 * slot, 1024), op=OPS.add), dg)
            b.w("sync", dg, out_marks[t])
            b.i("sync", lambda e_, t=t, slot=slot: e_.dma_start(
                out=out_h[128 * t:128 * t + 128, :], in_=F("outb", 1024 * slot, 1024)), qo[slot], 16)

        b.w("sync", qo[0], qo[0].n)
        b.w("sync", qo[1], qo[1].n)
        if DBG:
            _emit_dbg(nc, b, e, dbg, ab, af, AB, AF32, s2d, sems, qib)
        return _install(nc, b, e)


def _emit_dbg(nc, b, e, dbg, ab, af, AB, AF32, s2d, sems, qib_ref=None):
    qd = Sem(e(nc.semaphore("qd")))
    for nm in ("dv", "dt", "dsc", "dg"):
        if sems[nm].n > 0:
            b.w("sync", sems[nm], sems[nm].n)
    def _d(name, src):
        b.i("sync", lambda e_, src=src, name=name: e_.dma_start(out=dbg[name], in_=src), qd, 16)
    _d("d_y1", ab[:, AB["y1"]:AB["y1"] + 16384])
    _d("d_vaug", ab[:, AB["vaug"]:AB["vaug"] + 16768])
    _d("d_qp", ab[:, AB["qp"]:AB["qp"] + 2048])
    _d("d_ka", ab[:, AB["ka"]:AB["ka"] + 2048])
    _d("d_kb", ab[:, AB["kb"]:AB["kb"] + 2048])
    _d("d_ofm", ab[:, AB["ofm"]:AB["ofm"] + 16384])
    _d("d_s2", s2d)
    _d("d_st", af[:, AF32["stats"]:AF32["stats"] + 64])
    _d("d_svb", af[:, AF32["svb"]:AF32["svb"] + 1024])
    _d("d_bbb", af[:, AF32["bbb"]:AF32["bbb"] + 1024])
    _d("d_fb", af[:, AF32["fb"]:AF32["fb"] + 1024])
    _d("d_t1", af[:, AF32["t1"]:AF32["t1"] + 1024])
    b.i("sync", lambda e_: e_.dma_start(out=dbg["d_qi"], in_=qib_ref[:, :]), qd, 16)
    b.w("sync", qd, qd.n)


def _install(nc, b, e):
    block = e(nc.Block())

    @block.sync
    def _(eng):
        for f in b.ops["sync"]:
            f(eng)

    @block.tensor
    def _(eng):
        for f in b.ops["tensor"]:
            f(eng)

    @block.vector
    def _(eng):
        for f in b.ops["vector"]:
            f(eng)

    @block.scalar
    def _(eng):
        for f in b.ops["scalar"]:
            f(eng)

    @block.gpsimd
    def _(eng):
        for f in b.ops["gpsimd"]:
            f(eng)

    return nc


_NC_CACHE = None


def kernel(L0_attn_x, L0_attn_W_q, L0_attn_W_k, L0_attn_W_v, L0_attn_W_o,
           L0_ffn_weight_int8, L0_ffn_weight_scale, L0_ffn_bias):
    global _NC_CACHE
    x = np.ascontiguousarray(np.asarray(L0_attn_x, dtype=np.float32))
    wq = np.ascontiguousarray(np.asarray(L0_attn_W_q, dtype=np.float32))
    wk = np.ascontiguousarray(np.asarray(L0_attn_W_k, dtype=np.float32))
    wv = np.ascontiguousarray(np.asarray(L0_attn_W_v, dtype=np.float32))
    wo = np.ascontiguousarray(np.asarray(L0_attn_W_o, dtype=np.float32))
    wf8 = np.ascontiguousarray(np.asarray(L0_ffn_weight_int8, dtype=np.int8))
    wsc = np.ascontiguousarray(np.asarray(L0_ffn_weight_scale, dtype=np.float32).reshape(1, D))
    wb = np.ascontiguousarray(np.asarray(L0_ffn_bias, dtype=np.float32).reshape(1, D))
    cos_t, sin_t, identb, onesf = host_tables()

    if _NC_CACHE is None:
        _NC_CACHE = build_nc()
    nc = _NC_CACHE

    in_maps = []
    for b_ in range(8):
        in_maps.append(dict(
            x=np.ascontiguousarray(x[b_]), wq=wq, wk=wk, wv=wv, wo=wo,
            wf8=wf8, wsc=wsc, wb=wb,
            cost=cos_t, sint=sin_t, identb=identb, onesf=onesf))
    res = run_bass_kernel_spmd(nc, in_maps, core_ids=list(range(8)))
    if DBG:
        return res.results
    return np.stack([np.asarray(r["out"], dtype=np.float32) for r in res.results])


# version 06 (final)
# speedup vs baseline: 1.0211x; 1.0211x over previous
import numpy as np
import ml_dtypes
import concourse.bass as bass
import concourse.mybir as mybir
from concourse.alu_op_type import AluOpType
from concourse.bass_utils import run_bass_kernel_spmd
from contextlib import ExitStack

F32 = mybir.dt.float32
BF16 = mybir.dt.bfloat16
I8 = mybir.dt.int8
I32 = mybir.dt.int32
AF = mybir.ActivationFunctionType
AX = mybir.AxisListType
OPS = AluOpType

S, D, H, DK = 2048, 1024, 16, 64
STOP = 99
DBG = False
NC = 16
NP = 8
QT = 4
ENGS = ("sync", "tensor", "vector", "scalar", "gpsimd")
SHUF_SWAP = [j + 1 - 2 * (j % 2) for j in range(32)]


def host_tables():
    inv = 1.0 / 10000.0 ** (np.arange(0, DK, 2, dtype=np.float64) / DK)
    pos = np.arange(S, dtype=np.float64)
    th = pos[None, :] * inv[:, None]
    cos = np.repeat(np.cos(th), 2, axis=0)
    sin = np.repeat(np.sin(th), 2, axis=0)
    sgn = np.where(np.arange(DK) % 2 == 0, -1.0, 1.0)[:, None]
    cos_t = np.vstack([cos, cos]).astype(ml_dtypes.bfloat16)
    sin_t = np.vstack([sin * sgn, sin * sgn]).astype(ml_dtypes.bfloat16)
    identb = np.eye(128, dtype=ml_dtypes.bfloat16)
    onesf = np.ones((1, 128), dtype=np.float32)
    return cos_t, sin_t, identb, onesf


class Sem:
    def __init__(self, handle):
        self.h = handle
        self.n = 0


class Builder:
    def __init__(self):
        self.ops = {e: [] for e in ENGS}

    def w(self, eng, sem, val):
        if val <= 0:
            return
        h = sem.h
        self.ops[eng].append(lambda e, h=h, v=val: e.wait_ge(h, v))

    def ping(self, sems, prod_mark):
        dg = sems["dg"]
        dv = sems["dv"]
        self.w("gpsimd", dv, prod_mark)
        dg.n += 1
        h, v = dg.h, dg.n
        self.ops["gpsimd"].append(lambda e, h=h: e.sem_inc(h, 1))
        self.w("vector", dg, dg.n)

    def i(self, eng, fn, sem=None, k=1):
        if sem is None:
            self.ops[eng].append(fn)
            if eng == "vector":
                self.ops[eng].append(lambda e: e.drain())
            return None
        sem.n += k
        h, kk = sem.h, k
        self.ops[eng].append(lambda e, fn=fn, h=h, kk=kk: fn(e).then_inc(h, kk))
        if eng == "vector":
            self.ops[eng].append(lambda e: e.drain())
        return sem.n


def build_nc():
    nc = bass.Bass("TRN2", target_bir_lowering=False, debug=False)

    x_h = nc.dram_tensor("x", [S, D], F32, kind="ExternalInput").ap()
    wq_h = nc.dram_tensor("wq", [D, D], F32, kind="ExternalInput").ap()
    wk_h = nc.dram_tensor("wk", [D, D], F32, kind="ExternalInput").ap()
    wv_h = nc.dram_tensor("wv", [D, D], F32, kind="ExternalInput").ap()
    wo_h = nc.dram_tensor("wo", [D, D], F32, kind="ExternalInput").ap()
    wf8_h = nc.dram_tensor("wf8", [D, D], I8, kind="ExternalInput").ap()
    wsc_h = nc.dram_tensor("wsc", [1, D], F32, kind="ExternalInput").ap()
    wb_h = nc.dram_tensor("wb", [1, D], F32, kind="ExternalInput").ap()
    cos_h = nc.dram_tensor("cost", [128, S], BF16, kind="ExternalInput").ap()
    sin_h = nc.dram_tensor("sint", [128, S], BF16, kind="ExternalInput").ap()
    id_h = nc.dram_tensor("identb", [128, 128], BF16, kind="ExternalInput").ap()
    onesf_h = nc.dram_tensor("onesf", [1, 128], F32, kind="ExternalInput").ap()
    out_h = nc.dram_tensor("out", [S, D], F32, kind="ExternalOutput").ap()

    stg_q = nc.dram_tensor("stg_q", [D, D], BF16).ap()
    stg_k = nc.dram_tensor("stg_k", [D, D], BF16).ap()
    stg_v = nc.dram_tensor("stg_v", [D, D], BF16).ap()
    stg_o = nc.dram_tensor("stg_o", [D, D], BF16).ap()
    stg_f = nc.dram_tensor("stg_f", [D, D], BF16).ap()
    s2d = nc.dram_tensor("s2d", [S, D], F32).ap()
    dbg = {}
    if DBG:
        dbg["d_y1"] = nc.dram_tensor("d_y1", [128, 16384], BF16, kind="ExternalOutput").ap()
        dbg["d_vaug"] = nc.dram_tensor("d_vaug", [128, 16768], BF16, kind="ExternalOutput").ap()
        dbg["d_qp"] = nc.dram_tensor("d_qp", [128, 2048], BF16, kind="ExternalOutput").ap()
        dbg["d_ka"] = nc.dram_tensor("d_ka", [128, 2048], BF16, kind="ExternalOutput").ap()
        dbg["d_kb"] = nc.dram_tensor("d_kb", [128, 2048], BF16, kind="ExternalOutput").ap()
        dbg["d_ofm"] = nc.dram_tensor("d_ofm", [128, 16384], BF16, kind="ExternalOutput").ap()
        dbg["d_s2"] = nc.dram_tensor("d_s2", [S, D], F32, kind="ExternalOutput").ap()
        dbg["d_st"] = nc.dram_tensor("d_st", [128, 64], F32, kind="ExternalOutput").ap()
        dbg["d_svb"] = nc.dram_tensor("d_svb", [128, 1024], F32, kind="ExternalOutput").ap()
        dbg["d_bbb"] = nc.dram_tensor("d_bbb", [128, 1024], F32, kind="ExternalOutput").ap()
        dbg["d_qi"] = nc.dram_tensor("d_qi", [128, 1024], mybir.dt.int32, kind="ExternalOutput").ap()
        dbg["d_fb"] = nc.dram_tensor("d_fb", [128, 1024], F32, kind="ExternalOutput").ap()
        dbg["d_t1"] = nc.dram_tensor("d_t1", [128, 1024], F32, kind="ExternalOutput").ap()
    mxd = nc.dram_tensor("mxd", [128, 1], F32).ap()
    ccin = nc.dram_tensor("ccin", [1, 1], F32).ap()
    ccout = nc.dram_tensor("ccout", [1, 1], F32).ap()

    with ExitStack() as st:
        e = st.enter_context

        AB = {}
        cur = 0
        for name, ln in [("y1", 16384), ("vaug", 16768), ("cos", 2048), ("sin", 2048),
                         ("ka", 2048), ("kb", 2048), ("qp", 2048), ("wqp", 1024),
                         ("wkp", 1024), ("wbig", 8192), ("pt", 2048), ("scr", 2048),
                         ("ident", 128), ("ones128", 128), ("ofm", 16384)]:
            AB[name] = cur
            cur += ln
        ab = e(nc.sbuf_tensor("ab", [128, cur], BF16))

        def B(name, off, ln, p0=0, pn=128):
            return ab[p0:p0 + pn, AB[name] + off: AB[name] + off + ln]

        AF32 = {}
        cur = 0
        for name, ln in [("xb", 2048), ("s2b", 2048), ("t1", 1024), ("fb", 1024),
                         ("outb", 2048), ("svb", 1024), ("bbb", 1024),
                         ("srow", 3584), ("stats", 64)]:
            AF32[name] = cur
            cur += ln
        af = e(nc.sbuf_tensor("af", [128, cur], F32))

        def F(name, off, ln, p0=0, pn=128):
            return af[p0:p0 + pn, AF32[name] + off: AF32[name] + off + ln]

        qib = e(nc.sbuf_tensor("qib", [128, 1024], I32))


        PA = e(nc.psum_tensor("PA", [128, 1024], F32))
        PB = e(nc.psum_tensor("PB", [128, 1024], F32))
        PC = e(nc.psum_tensor("PC", [128, 512], F32))
        PD = e(nc.psum_tensor("PD", [128, 512], F32))
        PE = e(nc.psum_tensor("PE", [128, 512], F32))
        PT = e(nc.psum_tensor("PT", [128, 1024], BF16))

        sems = {}
        for s_ in ["q0", "qx0", "qx1", "qw", "qv", "qk", "qs2w", "qs2r0", "qs2r1",
                   "qo0", "qo1", "qm", "qwo", "qwf", "dt", "dv", "dsc", "dg", "ccs"]:
            sems[s_] = Sem(e(nc.semaphore(s_)))
        q0, qw, qv, qk = (sems[k] for k in ["q0", "qw", "qv", "qk"])
        qs2w, qm, qwo, qwf = (sems[k] for k in ["qs2w", "qm", "qwo", "qwf"])
        dt, dv, dsc, dg, ccs = (sems[k] for k in ["dt", "dv", "dsc", "dg", "ccs"])
        qx = [sems["qx0"], sems["qx1"]]
        qs2r = [sems["qs2r0"], sems["qs2r1"]]
        qo = [sems["qo0"], sems["qo1"]]

        b = Builder()

        eps_ap = F("stats", 17, 1)
        mva = F("stats", 0, 2)
        bna = F("stats", 2, 12)
        rinv = F("stats", 14, 1)
        nbv = F("stats", 15, 1)
        sdv = F("stats", 16, 1)
        onesf_row = F("srow", 2048, 128, pn=1)
        onesf_11 = F("srow", 2048, 1, pn=1)
        scrrow = F("srow", 2176, 1024, pn=1)


        def emit_ln_stats(src_fn, junk_name):
            b.i("vector", lambda e_: e_.tensor_reduce(
                out=mva[:, 0:1], in_=src_fn(), axis=AX.X, op=OPS.add))
            b.i("vector", lambda e_: e_.tensor_tensor(
                out=F(junk_name, 0, 1024), in0=src_fn(), in1=src_fn(), op=OPS.mult))
            b.i("vector", lambda e_: e_.tensor_reduce(
                out=mva[:, 1:2], in_=F(junk_name, 0, 1024), axis=AX.X, op=OPS.add))
            b.i("vector", lambda e_: e_.tensor_scalar(
                out=mva[:, 0:1], in0=mva[:, 0:1], scalar1=1.0 / 1024.0, scalar2=None,
                op0=OPS.mult, op1=OPS.bypass))
            b.i("vector", lambda e_: e_.tensor_scalar(
                out=mva[:, 1:2], in0=mva[:, 1:2], scalar1=1.0 / 1024.0, scalar2=None,
                op0=OPS.mult, op1=OPS.bypass))
            b.i("vector", lambda e_: e_.tensor_scalar(
                out=F("stats", 2, 1), in0=mva[:, 0:1], scalar1=-1.0, scalar2=None,
                op0=OPS.mult, op1=OPS.bypass))
            b.i("vector", lambda e_: e_.tensor_tensor(
                out=nbv, in0=mva[:, 0:1], in1=mva[:, 0:1], op=OPS.mult))
            return b.i("vector", lambda e_: e_.tensor_tensor(
                out=mva[:, 1:2], in0=mva[:, 1:2], in1=nbv, op=OPS.subtract), dv)

        def emit_ln_tail():
            b.i("vector", lambda e_: e_.reciprocal(out=rinv, in_=sdv))
            mk = b.i("vector", lambda e_: e_.tensor_tensor(
                out=nbv, in0=F("stats", 2, 1), in1=rinv, op=OPS.mult), dv)
            b.ping(sems, mk)

        b.i("sync", lambda e_: e_.dma_start(out=B("cos", 0, S), in_=cos_h), q0, 16)
        b.i("sync", lambda e_: e_.dma_start(out=B("sin", 0, S), in_=sin_h), q0, 16)
        b.i("sync", lambda e_: e_.dma_start(out=B("ident", 0, 128), in_=id_h), q0, 16)
        b.i("sync", lambda e_: e_.dma_start(out=onesf_row, in_=onesf_h), q0, 16)
        b.i("sync", lambda e_: e_.dma_start(out=F("srow", 0, 1024, pn=1), in_=wsc_h), q0, 16)
        b.i("sync", lambda e_: e_.dma_start(out=F("srow", 1024, 1024, pn=1), in_=wb_h), q0, 16)

        b.i("gpsimd", lambda e_: e_.dma_start(out=stg_q, in_=wq_h), qw, 16)
        b.i("gpsimd", lambda e_: e_.dma_start(out=stg_k, in_=wk_h), qw, 16)
        b.i("gpsimd", lambda e_: e_.dma_start(out=stg_v, in_=wv_h), qw, 16)
        b.i("gpsimd", lambda e_: e_.dma_start(out=stg_o, in_=wo_h), qw, 16)
        b.i("gpsimd", lambda e_: e_.dma_start(out=stg_f, in_=wf8_h), qw, 16)

        b.i("vector", lambda e_: e_.memset(B("ka", 0, 2048), 0.0))
        b.i("vector", lambda e_: e_.memset(B("kb", 0, 2048), 0.0))
        b.i("vector", lambda e_: e_.memset(B("vaug", 0, 16768), 0.0))
        vaug3 = ab[:, AB["vaug"]:AB["vaug"] + 16640].rearrange("p (t h c) -> p t h c", h=16, c=65)
        b.i("vector", lambda e_: e_.memset(vaug3[:, :, :, 64:65], 1.0))
        b.i("vector", lambda e_: e_.memset(B("ones128", 0, 128), 1.0))
        b.i("vector", lambda e_: e_.memset(F("stats", 17, 1), 1e-5))
        b.i("vector", lambda e_: e_.memset(B("scr", 512, 640), 0.0))
        b.i("vector", lambda e_: e_.memset(F("stats", 21, 1), 0.0), dv)

        b.w("sync", qw, 48)
        for c in range(8):
            b.i("sync", lambda e_, c=c: e_.dma_start_transpose(
                out=B("wbig", 1024 * c, 1024), in_=stg_v[:, 128 * c:128 * c + 128]), qv, 16)

        if STOP < 1:
            if DBG:
                _emit_dbg(nc, b, e, dbg, ab, af, AB, AF32, s2d, sems, qib)
            return _install(nc, b, e)
        norm_marks = {}
        evac_marks = {}
        vproj_marks = {}
        tp_marks = {}
        for t in range(NC):
            slot = t % 2
            if t >= 2:
                b.w("sync", dv, norm_marks[t - 2])
            b.i("sync", lambda e_, t=t, slot=slot: e_.dma_start(
                out=F("xb", 1024 * slot, 1024), in_=x_h[128 * t:128 * t + 128, :]), qx[slot], 16)

            b.w("vector", qx[slot], 16 * (t // 2 + 1))
            m_ag = emit_ln_stats(lambda slot=slot: F("xb", 1024 * slot, 1024), "svb")
            b.w("scalar", dv, m_ag)
            m_sq = b.i("scalar", lambda e_: e_.activation(out=sdv, in_=mva[:, 1:2], func=AF.Sqrt,
                                                          bias=eps_ap, scale=1.0), dsc)
            b.w("vector", dsc, m_sq)
            emit_ln_tail()
            if t >= 2:
                b.w("vector", dt, tp_marks[t - 2])
            norm_marks[t] = b.i("vector", lambda e_, slot=slot: e_.tensor_scalar(
                out=B("scr", 1024 * slot, 1024), in0=F("xb", 1024 * slot, 1024),
                scalar1=rinv[:, 0:1], scalar2=nbv[:, 0:1], op0=OPS.mult, op1=OPS.add), dv)

            b.w("tensor", dv, norm_marks[t])
            if t >= 1:
                b.w("tensor", dv, evac_marks[t - 1])
            m_tp = None
            for c in range(8):
                m_tp = b.i("tensor", lambda e_, c=c, slot=slot: e_.transpose(
                    PT[:, 128 * c:128 * c + 128], B("scr", 1024 * slot + 128 * c, 128),
                    B("ident", 0, 128)), dt)
            tp_marks[t] = m_tp
            b.w("vector", dt, m_tp)
            y1_dst = ab[:, AB["y1"]:AB["y1"] + 16384].rearrange(
                "p (c s) -> p c s", s=2048)[:, :, 128 * t:128 * t + 128]
            evac_marks[t] = b.i("vector", lambda e_, y1_dst=y1_dst: e_.tensor_copy(
                out=y1_dst, in_=PT[:, 0:1024].rearrange("p (c s) -> p c s", s=128)), dv)

            b.w("tensor", qv, 128)
            b.w("tensor", dv, evac_marks[t])
            if t >= 1:
                b.w("tensor", dv, vproj_marks[t - 1])
            mk = None
            for oh in range(2):
                bank = PC if oh == 0 else PD
                for c in range(8):
                    mk = b.i("tensor", lambda e_, bank=bank, c=c, t=t, oh=oh: e_.matmul(
                        bank[:, :],
                        ab[:, AB["y1"] + 2048 * c + 128 * t: AB["y1"] + 2048 * c + 128 * t + 128],
                        B("wbig", 1024 * c + 512 * oh, 512),
                        start=(c == 0), stop=(c == 7)), dt)
            b.w("vector", dt, mk)
            vdst0 = ab[:, AB["vaug"] + 1040 * t: AB["vaug"] + 1040 * t + 520].rearrange(
                "p (h c) -> p h c", c=65)[:, :, 0:64]
            vdst1 = ab[:, AB["vaug"] + 1040 * t + 520: AB["vaug"] + 1040 * t + 1040].rearrange(
                "p (h c) -> p h c", c=65)[:, :, 0:64]
            b.i("vector", lambda e_, vdst0=vdst0: e_.tensor_copy(out=vdst0, in_=PC[:, :]))
            vproj_marks[t] = b.i("vector", lambda e_, vdst1=vdst1: e_.tensor_copy(out=vdst1, in_=PD[:, :]), dv)

        y1_done = evac_marks[NC - 1]
        ph1_t_done = dt.n
        ph1_v_done = dv.n

        b.w("sync", dt, ph1_t_done)
        for c in range(8):
            b.i("sync", lambda e_, c=c: e_.dma_start_transpose(
                out=B("wbig", 1024 * c, 1024), in_=stg_o[:, 128 * c:128 * c + 128]), qwo, 16)

        if STOP < 2:
            if DBG:
                _emit_dbg(nc, b, e, dbg, ab, af, AB, AF32, s2d, sems, qib)
            return _install(nc, b, e)
        cva = F("stats", 22, 1)
        cvb = F("stats", 23, 1)
        rowmax = F("stats", 24, 1)
        rdt4 = F("stats", 28, 4)

        last_exp = {0: 0, 1: 0}
        last_ptil = {0: 0, 1: 0}
        last_obank = {0: ph1_v_done, 1: ph1_v_done}
        last_pt_free = ph1_v_done
        last_pe_free = ph1_v_done
        cv_marks = [0, 0]
        final_proj_done = 0

        b.w("sync", qw, 80)
        for wreg, stg in (("wqp", stg_q), ("wkp", stg_k)):
            for c in range(8):
                b.i("sync", lambda e_, c=c, wreg=wreg, stg=stg: e_.dma_start_transpose(
                    out=B(wreg, 128 * c, 128), in_=stg[0:128, 128 * c:128 * c + 128]), qk, 16)

        for p in range(NP):
            b.w("tensor", qk, 256 * (p + 1))
            b.w("tensor", dv, y1_done)
            b.w("tensor", dv, last_pe_free)
            for ti, wreg in (("q", "wqp"), ("k", "wkp")):
                for sl in range(4):
                    if dg.n > 0:
                        b.w("tensor", dg, dg.n)
                    mk = None
                    for c in range(8):
                        mk = b.i("tensor", lambda e_, c=c, sl=sl, wreg=wreg: e_.matmul(
                            PE[:, :], B(wreg, 128 * c, 128),
                            ab[:, AB["y1"] + 2048 * c + 512 * sl: AB["y1"] + 2048 * c + 512 * sl + 512],
                            start=(c == 0), stop=(c == 7)), dt)
                    b.w("vector", dt, mk)
                    b.i("vector", lambda e_, sl=sl: e_.tensor_tensor(
                        out=F("xb", 0, 512), in0=PE[:, :], in1=B("cos", 512 * sl, 512), op=OPS.mult))
                    b.i("vector", lambda e_: e_.stream_shuffle(
                        out=F("xb", 512, 512), in_=PE[:, :], mask=SHUF_SWAP))
                    mk2 = b.i("vector", lambda e_, sl=sl: e_.tensor_tensor(
                        out=F("xb", 1024, 512), in0=F("xb", 512, 512),
                        in1=B("sin", 512 * sl, 512), op=OPS.mult), dv)
                    b.w("gpsimd", dv, mk2)
                    if ti == "q":
                        b.i("gpsimd", lambda e_, sl=sl: e_.tensor_tensor(
                            out=B("qp", 512 * sl, 512), in0=F("xb", 0, 512),
                            in1=F("xb", 1024, 512), op=OPS.add), dg)
                    else:
                        b.i("gpsimd", lambda e_, sl=sl: e_.tensor_tensor(
                            out=B("ka", 512 * sl, 512, p0=0, pn=64),
                            in0=F("xb", 0, 512, p0=0, pn=64),
                            in1=F("xb", 1024, 512, p0=0, pn=64), op=OPS.add))
                        b.i("gpsimd", lambda e_, sl=sl: e_.tensor_tensor(
                            out=B("kb", 512 * sl, 512, p0=64, pn=64),
                            in0=F("xb", 0, 512, p0=64, pn=64),
                            in1=F("xb", 1024, 512, p0=64, pn=64), op=OPS.add), dg)
            proj_done = dg.n
            final_proj_done = proj_done

            if p + 1 < NP:
                b.w("sync", dg, proj_done)
                for wreg, stg in (("wqp", stg_q), ("wkp", stg_k)):
                    for c in range(8):
                        b.i("sync", lambda e_, c=c, wreg=wreg, stg=stg, p=p: e_.dma_start_transpose(
                            out=B(wreg, 128 * c, 128),
                            in_=stg[128 * (p + 1):128 * (p + 1) + 128, 128 * c:128 * c + 128]), qk, 16)

            for h01 in range(2):
                kreg = "ka" if h01 == 0 else "kb"
                cv_ = cva if h01 == 0 else cvb
                b.w("tensor", dg, proj_done)
                b.w("tensor", dv, last_pt_free)
                b.w("tensor", dv, last_pe_free)
                mk = b.i("tensor", lambda e_, kreg=kreg: e_.matmul(
                    PE[:, :], B(kreg, 0, 128), B("qp", 0, 512), start=True, stop=True), dt)
                b.w("vector", dt, mk)
                b.i("vector", lambda e_: e_.reduce_max(out=rowmax, in_=PE[:, :], axis=AX.X))
                mk2 = b.i("vector", lambda e_: e_.tensor_copy(out=B("scr", 1216, 1), in_=rowmax), dv)
                b.w("tensor", dv, mk2)
                mk3 = b.i("tensor", lambda e_: e_.transpose(
                    PT[0:1, 0:128], B("scr", 1216, 1), B("ident", 0, 128)), dt)
                b.w("vector", dt, mk3)
                mkc = b.i("vector", lambda e_: e_.reduce_max(out=F("stats", 25, 1, pn=1),
                                                              in_=PT[0:1, 0:128], axis=AX.X), dv)
                b.ping(sems, mkc)
                mk4 = b.i("vector", lambda e_: e_.tensor_scalar(
                    out=F("srow", 2176, 128, pn=1), in0=onesf_row,
                    scalar1=F("stats", 25, 1, pn=1), scalar2=None,
                    op0=OPS.mult, op1=OPS.bypass), dv)
                last_pt_free = mk4
                b.w("tensor", dv, mk4)
                mk5 = b.i("tensor", lambda e_: e_.transpose(
                    PE[:, 0:1], F("srow", 2176, 128, pn=1), onesf_11), dt)
                b.w("vector", dt, mk5)
                cv_marks[h01] = b.i("vector", lambda e_, cv_=cv_: e_.tensor_scalar(
                    out=cv_, in0=PE[:, 0:1], scalar1=-0.125, scalar2=-4.0,
                    op0=OPS.mult, op1=OPS.add), dv)
                last_pe_free = cv_marks[h01]

            for h01 in range(2):
                kreg = "ka" if h01 == 0 else "kb"
                cv_ = cva if h01 == 0 else cvb
                hh = 2 * p + h01
                for qt in range(QT):
                    ob = qt % 2
                    obank = PC if ob == 0 else PD
                    exp_marks = {}
                    pv_marks = {}
                    for g in range(8):
                        sb_i = g % 2
                        sbuf = PA if sb_i == 0 else PB
                        pbuf_off = 1024 * sb_i
                        b.w("tensor", dsc, last_exp[sb_i])
                        for half in range(2):
                            kc = 2 * g + half
                            b.i("tensor", lambda e_, sbuf=sbuf, half=half, kc=kc, kreg=kreg, qt=qt: e_.matmul(
                                sbuf[:, 512 * half:512 * half + 512],
                                B(kreg, 128 * kc, 128), B("qp", 512 * qt, 512),
                                start=True, stop=True), dt)
                        sc_mark = dt.n
                        b.w("scalar", dt, sc_mark)
                        b.w("scalar", dt, last_ptil[sb_i])
                        b.w("scalar", dv, cv_marks[h01])
                        exp_marks[g] = b.i("scalar", lambda e_, sbuf=sbuf, pbuf_off=pbuf_off, cv_=cv_: e_.activation(
                            out=B("pt", pbuf_off, 1024), in_=sbuf[:, :], func=AF.Exp,
                            bias=cv_[:, 0:1], scale=0.125), dsc)
                        last_exp[sb_i] = exp_marks[g]
                        b.w("tensor", dsc, exp_marks[g])
                        if g == 0:
                            b.w("tensor", dv, last_obank[ob])
                        for half in range(2):
                            kcc = 2 * g + half
                            b.i("tensor", lambda e_, obank=obank, kcc=kcc, hh=hh, pbuf_off=pbuf_off, half=half: e_.matmul(
                                obank[:, :],
                                ab[:, AB["vaug"] + 1040 * kcc + 65 * hh: AB["vaug"] + 1040 * kcc + 65 * hh + 128],
                                B("pt", pbuf_off + 512 * half, 512),
                                start=(kcc == 0), stop=(kcc == 15)), dt)
                        pv_marks[g] = dt.n
                        last_ptil[sb_i] = pv_marks[g]

                    b.w("vector", dt, pv_marks[7])
                    osb_mark = b.i("vector", lambda e_, obank=obank: e_.tensor_copy(
                        out=B("scr", 0, 512, pn=65), in_=obank[0:65, :]), dv)
                    last_obank[ob] = osb_mark
                    b.w("tensor", dv, osb_mark)
                    b.w("tensor", dv, last_pt_free)
                    mk = None
                    for j in range(4):
                        mk = b.i("tensor", lambda e_, j=j: e_.transpose(
                            PT[:, 128 * j:128 * j + 65], B("scr", 128 * j, 128, pn=65),
                            B("ident", 0, 128, pn=65)[:, 0:65]), dt)
                    b.w("vector", dt, mk)
                    ot_sl = PT[:, 0:512].rearrange("p (j c) -> p j c", c=128)[:, :, 64:65]
                    mkr = b.i("vector", lambda e_, ot_sl=ot_sl: e_.reciprocal(
                        out=rdt4.rearrange("p (j c) -> p j c", c=1), in_=ot_sl), dv)
                    b.ping(sems, mkr)
                    for j in range(4):
                        b.i("vector", lambda e_, j=j: e_.tensor_scalar(
                            out=B("scr", 1216 + 128 * j, 128), in0=B("ident", 0, 128),
                            scalar1=rdt4[:, j:j + 1], scalar2=None, op0=OPS.mult, op1=OPS.bypass))
                    mk2 = b.i("vector", lambda e_: e_.tensor_copy(
                        out=B("scr", 512 + 64, 512), in_=PT[:, 0:512]), dv)
                    last_pt_free = mk2
                    b.w("tensor", dv, mk2)
                    b.w("tensor", dv, last_pe_free)
                    mk3 = None
                    for j in range(4):
                        off = 512 + 64 + 128 * j - 64 * h01
                        mk3 = b.i("tensor", lambda e_, j=j, off=off: e_.matmul(
                            PE[:, 128 * j: 128 * j + 128],
                            B("scr", off, 128), B("scr", 1216 + 128 * j, 128),
                            start=True, stop=True), dt)
                    b.w("vector", dt, mk3)
                    po = 64 * h01
                    last_pe_free = b.i("vector", lambda e_, po=po, p=p, qt=qt: e_.tensor_copy(
                        out=ab[po:po + 64, AB["ofm"] + 2048 * p + 512 * qt: AB["ofm"] + 2048 * p + 512 * qt + 512],
                        in_=PE[po:po + 64, 0:512]), dv)

        attn_done_v = dv.n
        attn_done_t = dt.n

        if STOP < 3:
            if DBG:
                _emit_dbg(nc, b, e, dbg, ab, af, AB, AF32, s2d, sems, qib)
            return _install(nc, b, e)
        sm2 = F("stats", 32, 16)
        srv2 = F("stats", 48, 16)
        mx_ = F("stats", 18, 1)
        mn_ = F("stats", 19, 1)
        dev_ = F("stats", 20, 1)
        lmax_ = F("stats", 21, 1)

        b.w("tensor", qwo, 128)
        b.w("tensor", dv, attn_done_v)
        b.w("sync", dg, final_proj_done)
        ph3_marks = {}
        t1_marks = {}
        for t in range(NC):
            slot = t % 2
            if t >= 2:
                b.w("sync", dv, ph3_marks[t - 2])
            b.i("sync", lambda e_, t=t, slot=slot: e_.dma_start(
                out=F("xb", 1024 * slot, 1024), in_=x_h[128 * t:128 * t + 128, :]), qx[slot], 16)
            if t >= 1:
                b.w("tensor", dv, t1_marks[t - 1])
            mk = None
            for oh in range(2):
                bank = PC if oh == 0 else PD
                for c in range(8):
                    mk = b.i("tensor", lambda e_, bank=bank, c=c, t=t, oh=oh: e_.matmul(
                        bank[:, :],
                        ab[:, AB["ofm"] + 2048 * c + 128 * t: AB["ofm"] + 2048 * c + 128 * t + 128],
                        B("wbig", 1024 * c + 512 * oh, 512),
                        start=(c == 0), stop=(c == 7)), dt)
            b.w("vector", dt, mk)
            b.i("vector", lambda e_: e_.tensor_copy(out=F("t1", 0, 512), in_=PC[:, :]))
            t1_marks[t] = b.i("vector", lambda e_: e_.tensor_copy(out=F("t1", 512, 512), in_=PD[:, :]), dv)
            m_ag = emit_ln_stats(lambda: F("t1", 0, 1024), "svb")
            b.w("scalar", dv, m_ag)
            m_sq = b.i("scalar", lambda e_: e_.activation(out=sdv, in_=mva[:, 1:2], func=AF.Sqrt,
                                                          bias=eps_ap, scale=1.0), dsc)
            b.w("vector", dsc, m_sq)
            emit_ln_tail()
            b.i("vector", lambda e_: e_.tensor_scalar(
                out=F("t1", 0, 1024), in0=F("t1", 0, 1024), scalar1=rinv[:, 0:1], scalar2=nbv[:, 0:1],
                op0=OPS.mult, op1=OPS.add))
            b.w("vector", qx[slot], 16 * (qx[slot].n // 16))
            if t >= 2:
                b.w("vector", qs2w, 16 * (t - 1))
            b.i("vector", lambda e_, slot=slot: e_.tensor_tensor(
                out=F("s2b", 1024 * slot, 1024), in0=F("t1", 0, 1024),
                in1=F("xb", 1024 * slot, 1024), op=OPS.add))
            m_ag2 = emit_ln_stats(lambda slot=slot: F("s2b", 1024 * slot, 1024), "svb")
            b.w("scalar", dv, m_ag2)
            m_sq2 = b.i("scalar", lambda e_: e_.activation(out=sdv, in_=mva[:, 1:2], func=AF.Sqrt,
                                                           bias=eps_ap, scale=1.0), dsc)
            b.w("vector", dsc, m_sq2)
            emit_ln_tail()
            b.i("vector", lambda e_, t=t: e_.tensor_copy(out=sm2[:, t:t + 1], in_=mva[:, 0:1]))
            b.i("vector", lambda e_, t=t: e_.tensor_copy(out=srv2[:, t:t + 1], in_=rinv))
            b.i("vector", lambda e_, slot=slot: e_.tensor_scalar(
                out=F("svb", 0, 1024), in0=F("s2b", 1024 * slot, 1024),
                scalar1=rinv[:, 0:1], scalar2=nbv[:, 0:1], op0=OPS.mult, op1=OPS.add))
            b.i("vector", lambda e_: e_.reduce_max(out=dev_, in_=F("svb", 0, 1024), axis=AX.X,
                                                   apply_absolute_value=True))
            ph3_marks[t] = b.i("vector", lambda e_: e_.tensor_tensor(out=lmax_, in0=lmax_, in1=dev_, op=OPS.max), dv)
            b.w("sync", dv, ph3_marks[t])
            b.i("sync", lambda e_, t=t, slot=slot: e_.dma_start(
                out=s2d[128 * t:128 * t + 128, :], in_=F("s2b", 1024 * slot, 1024)), qs2w, 16)

        if STOP == 21:
            if DBG:
                _emit_dbg(nc, b, e, dbg, ab, af, AB, AF32, s2d, sems, qib)
            return _install(nc, b, e)
        b.w("sync", dv, ph3_marks[NC - 1])
        b.i("sync", lambda e_: e_.dma_start(out=mxd, in_=lmax_), qm, 16)
        b.w("sync", qm, 16)
        b.i("sync", lambda e_: e_.dma_start(out=F("srow", 3200, 128, pn=1),
                                            in_=mxd.rearrange("p one -> one p")), qm, 16)
        b.w("vector", qm, 32)
        b.i("vector", lambda e_: e_.reduce_max(out=F("stats", 25, 1, pn=1),
                                               in_=F("srow", 3200, 128, pn=1), axis=AX.X))
        mloc = b.i("vector", lambda e_: e_.tensor_scalar(
            out=F("stats", 25, 1, pn=1), in0=F("stats", 25, 1, pn=1),
            scalar1=1.0 / 127.0, scalar2=1e-5, op0=OPS.mult, op1=OPS.max), dv)
        b.w("sync", dv, mloc)
        b.i("sync", lambda e_: e_.dma_start(out=ccin, in_=F("stats", 25, 1, pn=1)), qm, 16)
        b.w("gpsimd", qm, 48)
        b.i("gpsimd", lambda e_: e_.collective_compute(
            "AllReduce", OPS.max, replica_groups=[list(range(8))],
            ins=[ccin.opt()], outs=[ccout.opt()]), ccs, 1)
        b.w("sync", ccs, 1)
        b.i("sync", lambda e_: e_.dma_start(out=F("stats", 25, 1, pn=1), in_=ccout), qm, 16)
        b.w("vector", qm, 64)
        mk = b.i("vector", lambda e_: e_.tensor_scalar(
            out=F("srow", 2176, 128, pn=1), in0=onesf_row,
            scalar1=F("stats", 25, 1, pn=1), scalar2=None, op0=OPS.mult, op1=OPS.bypass), dv)
        b.w("tensor", dv, mk)
        b.w("tensor", dv, attn_done_v)
        mk2 = b.i("tensor", lambda e_: e_.transpose(
            PE[:, 0:1], F("srow", 2176, 128, pn=1), onesf_11), dt)
        b.w("vector", dt, mk2)
        b.i("vector", lambda e_: e_.tensor_copy(out=F("stats", 26, 1), in_=PE[:, 0:1]))
        mkx = b.i("vector", lambda e_: e_.reciprocal(out=F("stats", 27, 1), in_=F("stats", 26, 1)), dv)
        b.ping(sems, mkx)

        if STOP == 22:
            if DBG:
                _emit_dbg(nc, b, e, dbg, ab, af, AB, AF32, s2d, sems, qib)
            return _install(nc, b, e)
        b.i("vector", lambda e_: e_.memset(B("scr", 1024, 1024), 0.0))
        sv_marks = {}
        for idx, dstname in ((0, "svb"), (1, "bbb")):
            if idx == 0:
                b.i("vector", lambda e_: e_.tensor_scalar(
                    out=scrrow, in0=F("srow", 0, 1024, pn=1),
                    scalar1=F("stats", 26, 1, pn=1), scalar2=None, op0=OPS.mult, op1=OPS.bypass))
            else:
                b.i("vector", lambda e_: e_.tensor_copy(out=scrrow, in_=F("srow", 1024, 1024, pn=1)))
            mk_hi = b.i("vector", lambda e_: e_.tensor_copy(out=B("scr", 1024, 1024, pn=1), in_=scrrow), dv)
            b.w("tensor", dv, mk_hi)
            if idx == 1:
                b.w("tensor", dv, sv_marks[0])
            mm_hi = None
            for oh in range(2):
                bank = PC if oh == 0 else PD
                mm_hi = b.i("tensor", lambda e_, bank=bank, oh=oh: e_.matmul(
                    bank[:, :], B("ones128", 0, 128), B("scr", 1024 + 512 * oh, 512),
                    start=True, stop=False), dt)
            b.w("vector", dt, mm_hi)
            b.i("vector", lambda e_: e_.tensor_tensor(
                out=scrrow, in0=scrrow, in1=B("scr", 1024, 1024, pn=1), op=OPS.subtract))
            mk_lo = b.i("vector", lambda e_: e_.tensor_copy(out=B("scr", 1024, 1024, pn=1), in_=scrrow), dv)
            b.w("tensor", dv, mk_lo)
            mm_lo = None
            for oh in range(2):
                bank = PC if oh == 0 else PD
                mm_lo = b.i("tensor", lambda e_, bank=bank, oh=oh: e_.matmul(
                    bank[:, :], B("ones128", 0, 128), B("scr", 1024 + 512 * oh, 512),
                    start=False, stop=True), dt)
            b.w("vector", dt, mm_lo)
            b.i("vector", lambda e_, dstname=dstname: e_.tensor_copy(out=F(dstname, 0, 512), in_=PC[:, :]))
            sv_marks[idx] = b.i("vector", lambda e_, dstname=dstname: e_.tensor_copy(
                out=F(dstname, 512, 512), in_=PD[:, :]), dv)

        sv_done = sv_marks[1]

        b.w("sync", dt, dt.n)
        for c in range(8):
            b.i("sync", lambda e_, c=c: e_.dma_start_transpose(
                out=B("wbig", 1024 * c, 1024), in_=stg_f[:, 128 * c:128 * c + 128]), qwf, 16)

        if STOP < 4:
            return _install(nc, b, e)
        qsc = F("stats", 24, 1)
        qbs = F("stats", 22, 1)
        qf_marks = {}
        ffn_mm_marks = {}
        stt_marks = {}
        out_marks = {}
        qt_marks = {}
        b.w("sync", qs2w, 256)
        for t in range(NC):
            slot = t % 2
            if t >= 2:
                b.w("sync", dg, out_marks[t - 2])
            b.i("sync", lambda e_, t=t, slot=slot: e_.dma_start(
                out=F("s2b", 1024 * slot, 1024), in_=s2d[128 * t:128 * t + 128, :]), qs2r[slot], 16)
            b.w("vector", qs2r[slot], 16 * (t // 2 + 1))
            b.w("vector", dv, sv_done)
            b.i("vector", lambda e_, t=t: e_.tensor_tensor(out=qsc, in0=srv2[:, t:t + 1],
                                                           in1=F("stats", 27, 1), op=OPS.mult))
            mkq = b.i("vector", lambda e_, t=t: e_.tensor_tensor(out=qbs, in0=sm2[:, t:t + 1],
                                                                 in1=qsc, op=OPS.mult), dv)
            b.ping(sems, mkq)
            b.i("vector", lambda e_, slot=slot: e_.tensor_scalar(
                out=qib[:, :], in0=F("s2b", 1024 * slot, 1024), scalar1=qsc[:, 0:1],
                scalar2=qbs[:, 0:1], op0=OPS.mult, op1=OPS.subtract))
            if t >= 1:
                b.w("vector", dt, qt_marks[t - 1])
            mk = b.i("vector", lambda e_: e_.tensor_copy(out=B("scr", 0, 1024), in_=qib[:, :]), dv)
            b.w("tensor", dv, mk)
            if t >= 1:
                b.w("tensor", dv, qf_marks[t - 1])
            mk2 = None
            for c in range(8):
                mk2 = b.i("tensor", lambda e_, c=c: e_.transpose(
                    PT[:, 128 * c:128 * c + 128], B("scr", 128 * c, 128), B("ident", 0, 128)), dt)
            qt_marks[t] = mk2
            b.w("vector", dt, mk2)
            if t >= 2:
                b.w("vector", dt, ffn_mm_marks[t - 2])
            qf_marks[t] = b.i("vector", lambda e_, slot=slot: e_.tensor_copy(
                out=B("pt", 1024 * slot, 1024), in_=PT[:, 0:1024]), dv)
            b.w("tensor", dv, qf_marks[t])
            b.w("tensor", qwf, 128)
            if t == 0:
                b.w("tensor", dv, sv_done)
            if t >= 1:
                b.w("tensor", dv, stt_marks[t - 1])
            mk3 = None
            for oh in range(2):
                bank = PC if oh == 0 else PD
                for c in range(8):
                    mk3 = b.i("tensor", lambda e_, bank=bank, c=c, oh=oh, slot=slot: e_.matmul(
                        bank[:, :], B("pt", 1024 * slot + 128 * c, 128),
                        B("wbig", 1024 * c + 512 * oh, 512),
                        start=(c == 0), stop=(c == 7)), dt)
            ffn_mm_marks[t] = mk3
            b.w("vector", dt, mk3)
            if t >= 1:
                b.w("vector", dg, out_marks[t - 1])
            b.i("vector", lambda e_: e_.scalar_tensor_tensor(
                out=F("fb", 0, 512), in0=PC[:, :], scalar=1.0, in1=F("svb", 0, 512),
                op0=OPS.mult, op1=OPS.mult))
            stt_marks[t] = b.i("vector", lambda e_: e_.scalar_tensor_tensor(
                out=F("fb", 512, 512), in0=PD[:, :], scalar=1.0, in1=F("svb", 512, 512),
                op0=OPS.mult, op1=OPS.mult), dv)
            b.w("gpsimd", dv, stt_marks[t])
            mk5 = b.i("gpsimd", lambda e_: e_.tensor_tensor(
                out=F("t1", 0, 1024), in0=F("fb", 0, 1024), in1=F("bbb", 0, 1024), op=OPS.add), dg)
            b.w("vector", dg, mk5)
            m_ag = emit_ln_stats(lambda: F("t1", 0, 1024), "xb")
            b.w("scalar", dv, m_ag)
            m_sq = b.i("scalar", lambda e_: e_.activation(out=sdv, in_=mva[:, 1:2], func=AF.Sqrt,
                                                          bias=eps_ap, scale=1.0), dsc)
            b.w("vector", dsc, m_sq)
            emit_ln_tail()
            mk6 = b.i("vector", lambda e_: e_.tensor_scalar(
                out=F("fb", 0, 1024), in0=F("t1", 0, 1024), scalar1=rinv[:, 0:1],
                scalar2=nbv[:, 0:1], op0=OPS.mult, op1=OPS.add), dv)
            b.w("gpsimd", dv, mk6)
            if t >= 2:
                b.w("gpsimd", qo[slot], 16 * (t // 2))
            out_marks[t] = b.i("gpsimd", lambda e_, slot=slot: e_.tensor_tensor(
                out=F("outb", 1024 * slot, 1024), in0=F("fb", 0, 1024),
                in1=F("s2b", 1024 * slot, 1024), op=OPS.add), dg)
            b.w("sync", dg, out_marks[t])
            b.i("sync", lambda e_, t=t, slot=slot: e_.dma_start(
                out=out_h[128 * t:128 * t + 128, :], in_=F("outb", 1024 * slot, 1024)), qo[slot], 16)

        b.w("sync", qo[0], qo[0].n)
        b.w("sync", qo[1], qo[1].n)
        if DBG:
            _emit_dbg(nc, b, e, dbg, ab, af, AB, AF32, s2d, sems, qib)
        return _install(nc, b, e)


def _emit_dbg(nc, b, e, dbg, ab, af, AB, AF32, s2d, sems, qib_ref=None):
    qd = Sem(e(nc.semaphore("qd")))
    for nm in ("dv", "dt", "dsc", "dg"):
        if sems[nm].n > 0:
            b.w("sync", sems[nm], sems[nm].n)
    def _d(name, src):
        b.i("sync", lambda e_, src=src, name=name: e_.dma_start(out=dbg[name], in_=src), qd, 16)
    _d("d_y1", ab[:, AB["y1"]:AB["y1"] + 16384])
    _d("d_vaug", ab[:, AB["vaug"]:AB["vaug"] + 16768])
    _d("d_qp", ab[:, AB["qp"]:AB["qp"] + 2048])
    _d("d_ka", ab[:, AB["ka"]:AB["ka"] + 2048])
    _d("d_kb", ab[:, AB["kb"]:AB["kb"] + 2048])
    _d("d_ofm", ab[:, AB["ofm"]:AB["ofm"] + 16384])
    _d("d_s2", s2d)
    _d("d_st", af[:, AF32["stats"]:AF32["stats"] + 64])
    _d("d_svb", af[:, AF32["svb"]:AF32["svb"] + 1024])
    _d("d_bbb", af[:, AF32["bbb"]:AF32["bbb"] + 1024])
    _d("d_fb", af[:, AF32["fb"]:AF32["fb"] + 1024])
    _d("d_t1", af[:, AF32["t1"]:AF32["t1"] + 1024])
    b.i("sync", lambda e_: e_.dma_start(out=dbg["d_qi"], in_=qib_ref[:, :]), qd, 16)
    b.w("sync", qd, qd.n)


def _install(nc, b, e):
    block = e(nc.Block())

    @block.sync
    def _(eng):
        for f in b.ops["sync"]:
            f(eng)

    @block.tensor
    def _(eng):
        for f in b.ops["tensor"]:
            f(eng)

    @block.vector
    def _(eng):
        for f in b.ops["vector"]:
            f(eng)

    @block.scalar
    def _(eng):
        for f in b.ops["scalar"]:
            f(eng)

    @block.gpsimd
    def _(eng):
        for f in b.ops["gpsimd"]:
            f(eng)

    return nc


_NC_CACHE = None


def kernel(L0_attn_x, L0_attn_W_q, L0_attn_W_k, L0_attn_W_v, L0_attn_W_o,
           L0_ffn_weight_int8, L0_ffn_weight_scale, L0_ffn_bias):
    global _NC_CACHE
    x = np.ascontiguousarray(np.asarray(L0_attn_x, dtype=np.float32))
    wq = np.ascontiguousarray(np.asarray(L0_attn_W_q, dtype=np.float32))
    wk = np.ascontiguousarray(np.asarray(L0_attn_W_k, dtype=np.float32))
    wv = np.ascontiguousarray(np.asarray(L0_attn_W_v, dtype=np.float32))
    wo = np.ascontiguousarray(np.asarray(L0_attn_W_o, dtype=np.float32))
    wf8 = np.ascontiguousarray(np.asarray(L0_ffn_weight_int8, dtype=np.int8))
    wsc = np.ascontiguousarray(np.asarray(L0_ffn_weight_scale, dtype=np.float32).reshape(1, D))
    wb = np.ascontiguousarray(np.asarray(L0_ffn_bias, dtype=np.float32).reshape(1, D))
    cos_t, sin_t, identb, onesf = host_tables()

    if _NC_CACHE is None:
        _NC_CACHE = build_nc()
    nc = _NC_CACHE

    in_maps = []
    for b_ in range(8):
        in_maps.append(dict(
            x=np.ascontiguousarray(x[b_]), wq=wq, wk=wk, wv=wv, wo=wo,
            wf8=wf8, wsc=wsc, wb=wb,
            cost=cos_t, sint=sin_t, identb=identb, onesf=onesf))
    res = run_bass_kernel_spmd(nc, in_maps, core_ids=list(range(8)))
    if DBG:
        return res.results
    return np.stack([np.asarray(r["out"], dtype=np.float32) for r in res.results])
